# version 05 (final)
# baseline (speedup 1.0000x reference)
import numpy as np
import ml_dtypes
import concourse.bass as bass
import concourse.mybir as mybir
from concourse.bass_utils import run_bass_kernel_spmd

F32 = mybir.dt.float32
BF16 = mybir.dt.bfloat16
AF = mybir.ActivationFunctionType
ALU = mybir.AluOpType
AX = mybir.AxisListType

B, T, D = 8, 4096, 1024
NH, HD = 16, 64
DFF, NE = 2048, 8
THETA = 10000.0
EPS = 1e-5
NEG = -100000.0


_ISSUER = {"QSP": "SP", "QACT": "ACT", "QPOOL": "POOL"}


class _Op:
    __slots__ = ("stream", "emit", "reads", "writes", "deps", "inc", "seq", "idx",
                 "epoch")

    def __init__(self, stream, emit, reads, writes):
        self.stream = stream
        self.emit = emit
        self.reads = reads
        self.writes = writes
        self.deps = {}
        self.inc = False
        self.seq = 0
        self.idx = 0


class Prog:
    def __init__(self):
        self.ops = []
        self._w = {}
        self._r = {}
        self._bar = {}
        self._epoch = 0

    def op(self, stream, emit, reads=(), writes=()):
        o = _Op(stream, emit, tuple(reads), tuple(writes))
        o.idx = len(self.ops)
        o.epoch = self._epoch
        self.ops.append(o)
        dep = dict(self._bar)
        for b in o.reads:
            wi = self._w.get(b)
            if wi is not None:
                s = self.ops[wi].stream
                if dep.get(s, -1) < wi:
                    dep[s] = wi
        for b in o.writes:
            wi = self._w.get(b)
            if wi is not None:
                s = self.ops[wi].stream
                if dep.get(s, -1) < wi:
                    dep[s] = wi
            for s, ri in self._r.get(b, {}).items():
                if dep.get(s, -1) < ri:
                    dep[s] = ri
        if stream == "PE":
            dep.pop(stream, None)
        o.deps = dep
        for b in o.reads:
            self._r.setdefault(b, {})[stream] = o.idx
        for b in o.writes:
            self._w[b] = o.idx
            self._r[b] = {}
        return o

    def barrier(self):
        self._epoch += 1
        last = {}
        for o in self.ops:
            last[o.stream] = o.idx
        self._bar = last

    def emit_all(self, nc):
        for o in self.ops:
            if o.stream.startswith("Q"):
                o.inc = True
            for s, pi in o.deps.items():
                self.ops[pi].inc = True
        cnt = {}
        for o in self.ops:
            if o.inc:
                amt = 16 if o.stream.startswith("Q") else 1
                cnt[o.stream] = cnt.get(o.stream, 0) + amt
                o.seq = cnt[o.stream]
            else:
                o.seq = cnt.get(o.stream, 0)
        sems = {}
        from contextlib import ExitStack
        ctx = ExitStack()
        for s in sorted(cnt):
            sems[s] = ctx.enter_context(nc.semaphore("sem_" + s))
        by_eng = {}
        for o in self.ops:
            eng = _ISSUER.get(o.stream, o.stream)
            by_eng.setdefault(eng, []).append(o)
        ops_map = self.ops

        def body(eng_name):
            def run(eng):
                hi = {}
                for o in by_eng[eng_name]:
                    for s, pi in sorted(o.deps.items()):
                        v = ops_map[pi].seq
                        if v > hi.get(s, 0):
                            eng.wait_ge(sems[s], v)
                            hi[s] = v
                    inst = o.emit(eng)
                    if o.inc:
                        amt = 16 if o.stream.startswith("Q") else 1
                        inst.then_inc(sems[o.stream], amt)
            return run

        with nc.Block() as block:
            if "PE" in by_eng:
                block.tensor(body("PE"))
            if "ACT" in by_eng:
                block.scalar(body("ACT"))
            if "DVE" in by_eng:
                block.vector(body("DVE"))
            if "POOL" in by_eng:
                block.gpsimd(body("POOL"))
            if "SP" in by_eng:
                block.sync(body("SP"))
        return ctx


class Arena:

    def __init__(self, ap, persist_end=0):
        self.ap = ap
        self.off = persist_end

    def take(self, nelem, dtype):
        nb = nelem * (2 if dtype == BF16 else 4)
        off = (self.off + 63) & ~63
        self.off = off + nb
        return self.ap[:, off:off + nb].bitcast(dtype)

    def mark(self):
        return self.off

    def reset(self, off):
        self.off = off



def _rope_tables():
    freqs = THETA ** (-np.arange(0, HD, 2, dtype=np.float64) / HD)
    ang = np.arange(T, dtype=np.float64)[:, None] * freqs[None, :]
    cos_t = np.cos(ang)
    sin_t = np.sin(ang)
    cosT = np.empty((128, T), np.float32)
    sinT = np.empty((128, T), np.float32)
    for p in range(128):
        hd = p % HD
        i = hd // 2
        cosT[p] = cos_t[:, i]
        sinT[p] = (-sin_t[:, i]) if hd % 2 == 0 else sin_t[:, i]
    return cosT, sinT


def _masks():
    jl = np.arange(128)[:, None]
    il = np.arange(512)[None, :]
    mask2 = np.stack([
        np.where(jl + 128 * r > il, np.float32(NEG), np.float32(0.0))
        for r in range(4)
    ]).astype(np.float32)
    return mask2


def _bias_heads(b):
    return np.ascontiguousarray(b.reshape(NH, HD).T.astype(np.float32))



def build_program(taps=()):
    nc = bass.Bass("TRN2", target_bir_lowering=False)
    P = Prog()

    def dram(name, shape, dt, kind="ExternalInput"):
        return nc.dram_tensor(name, list(shape), dt, kind=kind).ap()

    d_x = dram("x_tm", (T, D), F32)
    d_xT = dram("xT", (D, T), F32)
    d_wqT = dram("wqT", (D, D), F32)
    d_wqsT = dram("wqsT", (D, D), F32)
    d_wkT = dram("wkT", (D, D), F32)
    d_wksT = dram("wksT", (D, D), F32)
    d_wvT = dram("wvT", (D, D), F32)
    d_woT = dram("woT", (D, D), F32)
    d_bqh = dram("bqh", (HD, NH), F32)
    d_bqsh = dram("bqsh", (HD, NH), F32)
    d_bkh = dram("bkh", (HD, NH), F32)
    d_bksh = dram("bksh", (HD, NH), F32)
    d_bv = dram("bv_b", (128, D), F32)
    d_bo = dram("bo_b", (128, D), F32)
    d_scale = dram("scale_rep", (128, 1), F32)
    d_gwT = dram("gwT", (D, NE), F32)
    d_w1T = dram("w1T", (NE, D, DFF), F32)
    d_w3T = dram("w3T", (NE, D, DFF), F32)
    d_w2T = dram("w2T", (NE, DFF, D), F32)
    d_cos = dram("cosT", (128, T), BF16)
    d_sin = dram("sinT", (128, T), BF16)
    d_mask2 = dram("mask2", (4, 128, 512), BF16)
    d_ident = dram("ident", (128, 128), F32)
    d_identbf = dram("identbf", (128, 128), BF16)

    d_out = dram("out", (T, D), F32, kind="ExternalOutput")

    from contextlib import ExitStack
    ctx = ExitStack()
    arena_t = ctx.enter_context(nc.sbuf_tensor("arena", [128, 207 * 1024], mybir.dt.uint8))
    ar = Arena(arena_t.ap())

    scl = ar.take(1, F32)
    gw = ar.take(64, BF16)
    wts = ar.take(128, F32)
    c_eps = ar.take(1, F32)
    c_invD = ar.take(1, F32)
    c_mhalf = ar.take(1, F32)
    bvb = ar.take(D, F32)
    bob = ar.take(D, F32)
    bqh = ar.take(NH, F32)
    bqsh = ar.take(NH, F32)
    bkh = ar.take(NH, F32)
    bksh = ar.take(NH, F32)
    bq_s = ar.take(NH, F32)
    bqs_s = ar.take(NH, F32)
    mask2 = ar.take(4 * 512, BF16)
    ident = ar.take(128, F32)
    identbf = ar.take(128, BF16)
    ones64 = ar.take(64, F32)

    P.op("QSP", lambda e: e.dma_start(out=scl, in_=d_scale), writes=["scl"])
    P.op("QPOOL", lambda e: e.dma_start(out=gw, in_=d_gwT.rearrange("(dc p) e -> p dc e", p=128)),
         writes=["gw"])
    P.op("DVE", lambda e: e.memset(c_eps, EPS), writes=["c_eps"])
    P.op("DVE", lambda e: e.memset(c_invD, 1.0 / D), writes=["c_invD"])
    P.op("DVE", lambda e: e.memset(c_mhalf, -0.5), writes=["c_mhalf"])
    P.op("DVE", lambda e: e.memset(ones64, 1.0), writes=["ones64"])
    for dst, src, nm in [(bvb, d_bv, "bvb"), (bob, d_bo, "bob"),
                         (bqh[0:64, :], d_bqh, "bqh"), (bqsh[0:64, :], d_bqsh, "bqsh"),
                         (bkh[0:64, :], d_bkh, "bkh"), (bksh[0:64, :], d_bksh, "bksh"),
                         (ident, d_ident, "ident"), (identbf, d_identbf, "identbf")]:
        P.op("QSP", lambda e, dst=dst, src=src: e.dma_start(out=dst, in_=src),
             writes=[nm])
    P.op("QSP", lambda e: e.dma_start(out=mask2, in_=d_mask2.rearrange("r p c -> p r c")),
         writes=["mask2"])
    P.op("DVE", lambda e: e.tensor_scalar_mul(bq_s[0:64, :], bqh[0:64, :], scl[0:64, :]),
         reads=["bqh", "scl"], writes=["bq_s"])
    P.op("DVE", lambda e: e.tensor_scalar_mul(bqs_s[0:64, :], bqsh[0:64, :], scl[0:64, :]),
         reads=["bqsh", "scl"], writes=["bqs_s"])

    mX = ar.mark()
    xTb = ar.take(8 * T, BF16)
    mY = ar.mark()
    oT = ar.take(8 * T, BF16)
    mZ = ar.mark()

    cosb = ar.take(T, BF16)
    sinb = ar.take(T, BF16)
    q_h = ar.take(T, BF16)
    k_h = ar.take(T, BF16)
    v_hT = ar.take(32 * 65, BF16)
    Psb = [ar.take(2048, BF16) for _ in range(2)]
    oacc = ar.take(512, F32)
    rrow = ar.take(512, F32)
    evA = ar.take(512, F32)
    evB = ar.take(512, F32)
    wqb = [ar.take(8 * HD, BF16) for _ in range(2)]
    wqsb = [ar.take(8 * HD, BF16) for _ in range(2)]
    wkb = [ar.take(8 * HD, BF16) for _ in range(2)]
    wksb = [ar.take(8 * HD, BF16) for _ in range(2)]
    wvb = [ar.take(8 * HD, BF16) for _ in range(2)]
    assert ar.mark() - mZ <= 64 * 1024, ar.mark() - mZ

    pP1 = ctx.enter_context(nc.psum_tensor("pP1", [128, 1024], F32))
    pP2 = ctx.enter_context(nc.psum_tensor("pP2", [128, 2048], F32))
    pPA = ctx.enter_context(nc.psum_tensor("pPA", [128, 512], F32))
    pPB = ctx.enter_context(nc.psum_tensor("pPB", [128, 512], F32))
    p1a, p1b = pP1[:, 0:512], pP1[:, 512:1024]
    p2s = [pP2[:, i * 512:(i + 1) * 512] for i in range(4)]

    P.op("QPOOL", lambda e: e.dma_start(out=xTb, in_=d_xT.rearrange("(dc p) t -> p dc t", p=128)),
         writes=["xTb"])
    P.op("QSP", lambda e: e.dma_start(out=cosb, in_=d_cos), writes=["cosb"])
    P.op("QSP", lambda e: e.dma_start(out=sinb, in_=d_sin), writes=["sinb"])
    P.op("DVE", lambda e: e.memset(v_hT, 1.0), writes=["v_ones"])

    def load_head_w(h):
        sl = h % 2
        for wbuf, src, nm in [(wqb[sl], d_wqT, "wq"), (wqsb[sl], d_wqsT, "wqs"),
                              (wkb[sl], d_wkT, "wk"), (wksb[sl], d_wksT, "wks"),
                              (wvb[sl], d_wvT, "wv")]:
            P.op("QPOOL", lambda e, wbuf=wbuf, src=src, h=h: e.dma_start(
                out=wbuf, in_=src[:, h * HD:(h + 1) * HD]
                .rearrange("(dc p) j -> p dc j", p=128)),
                writes=[f"{nm}{sl}"])

    load_head_w(0)
    for h in range(NH):
        sl = h % 2
        if h + 1 < NH:
            load_head_w(h + 1)
            slack = [f"wq{1 - sl}", f"wv{1 - sl}"]
        else:
            P.op("QPOOL", lambda e: e.dma_start(
                out=gw, in_=d_gwT.rearrange("(dc p) e -> p dc e", p=128)),
                writes=["gwd1"])
            slack = ["gwd1"]
        for tc in range(8):
            for pb, pbn, wbuf, wn in [(p1a[0:64, :], "p1a", wkb[sl], f"wk{sl}"),
                                      (p1b[0:64, :], "p1b", wksb[sl], f"wks{sl}"),
                                      (p2s[0][0:64, :], "p2a", wqb[sl], f"wq{sl}"),
                                      (p2s[1][0:64, :], "p2b", wqsb[sl], f"wqs{sl}")]:
                for dc in range(8):
                    P.op("PE", lambda e, pb=pb, wbuf=wbuf, dc=dc, tc=tc: e.matmul(
                        pb, wbuf[:, dc * HD:(dc + 1) * HD],
                        xTb[:, dc * T + tc * 512: dc * T + tc * 512 + 512],
                        start=(dc == 0), stop=(dc == 7), skip_group_check=True),
                        reads=["xTb", wn] + slack, writes=[pbn])
            P.op("ACT", lambda e, h=h: e.activation(
                evA[0:64, :], p1a[0:64, :], AF.Identity, bias=bkh[0:64, h:h + 1]),
                reads=["p1a", "bkh"], writes=["evA"])
            P.op("DVE", lambda e, tc=tc: e.tensor_tensor(
                evA[0:64, :], evA[0:64, :], cosb[0:64, tc * 512:(tc + 1) * 512],
                op=ALU.mult), reads=["evA", "cosb"], writes=["evA"])
            P.op("ACT", lambda e, h=h: e.activation(
                evB[0:64, :], p1b[0:64, :], AF.Identity, bias=bksh[0:64, h:h + 1]),
                reads=["p1b", "bksh"], writes=["evB"])
            P.op("DVE", lambda e, tc=tc: e.tensor_tensor(
                evB[0:64, :], evB[0:64, :], sinb[0:64, tc * 512:(tc + 1) * 512],
                op=ALU.mult), reads=["evB", "sinb"], writes=["evB"])
            P.op("DVE", lambda e, tc=tc: e.tensor_tensor(
                k_h[0:64, tc * 512:(tc + 1) * 512], evA[0:64, :], evB[0:64, :],
                op=ALU.add), reads=["evA", "evB"], writes=["k_h"])
            P.op("ACT", lambda e, h=h: e.activation(
                evA[0:64, :], p2s[0][0:64, :], AF.Identity,
                bias=bq_s[0:64, h:h + 1], scale=scl[0:64, :]),
                reads=["p2a", "bq_s", "scl"], writes=["evA"])
            P.op("DVE", lambda e, tc=tc: e.tensor_tensor(
                evA[0:64, :], evA[0:64, :], cosb[0:64, tc * 512:(tc + 1) * 512],
                op=ALU.mult), reads=["evA", "cosb"], writes=["evA"])
            P.op("ACT", lambda e, h=h: e.activation(
                evB[0:64, :], p2s[1][0:64, :], AF.Identity,
                bias=bqs_s[0:64, h:h + 1], scale=scl[0:64, :]),
                reads=["p2b", "bqs_s", "scl"], writes=["evB"])
            P.op("DVE", lambda e, tc=tc: e.tensor_tensor(
                evB[0:64, :], evB[0:64, :], sinb[0:64, tc * 512:(tc + 1) * 512],
                op=ALU.mult), reads=["evB", "sinb"], writes=["evB"])
            P.op("DVE", lambda e, tc=tc: e.tensor_tensor(
                q_h[0:64, tc * 512:(tc + 1) * 512], evA[0:64, :], evB[0:64, :],
                op=ALU.add), reads=["evA", "evB"], writes=["q_h"])
        for t128 in range(32):
            pb = p1a[:, 0:64] if t128 % 2 == 0 else p1b[:, 0:64]
            pbn = "p1a" if t128 % 2 == 0 else "p1b"
            for dc in range(8):
                P.op("PE", lambda e, pb=pb, dc=dc, t128=t128, wv=wvb[sl]: e.matmul(
                    pb, xTb[:, dc * T + t128 * 128: dc * T + t128 * 128 + 128],
                    wv[:, dc * HD:(dc + 1) * HD],
                    start=(dc == 0), stop=(dc == 7), skip_group_check=True),
                    reads=["xTb", f"wv{sl}"] + slack, writes=[pbn])
            P.op("DVE", lambda e, pb=pb, t128=t128, h=h: e.tensor_tensor(
                v_hT[:, t128 * 65: t128 * 65 + 64], pb,
                bvb[:, h * HD:(h + 1) * HD], op=ALU.add),
                reads=[pbn, "bvb"], writes=["v_hT"])
        for ic in range(8):
            ng = ic + 1
            for g in range(ng):
                psl = g % 2
                avp, avn = (pPA, "pPA") if g % 2 == 0 else (pPB, "pPB")
                for k in range(4):
                    jt = g * 4 + k
                    P.op("PE", lambda e, jt=jt, k=k, ic=ic: e.matmul(
                        p2s[k], k_h[0:64, jt * 128:(jt + 1) * 128],
                        q_h[0:64, ic * 512:(ic + 1) * 512], start=True, stop=True,
                        skip_group_check=True),
                        reads=["k_h", "q_h"], writes=[f"p2{'abcd'[k]}"])
                    if g == ng - 1:
                        P.op("DVE", lambda e, k=k: e.tensor_tensor(
                            p2s[k], p2s[k], mask2[:, k * 512:(k + 1) * 512], op=ALU.add),
                            reads=[f"p2{'abcd'[k]}", "mask2"], writes=[f"p2{'abcd'[k]}"])
                P.op("ACT", lambda e, psl=psl: e.activation(
                    Psb[psl], pP2[:, :], AF.Exp),
                    reads=["p2a", "p2b", "p2c", "p2d"], writes=[f"Psb{psl}"])
                for k in range(4):
                    jt = g * 4 + k
                    P.op("PE", lambda e, jt=jt, k=k, psl=psl, avp=avp: e.matmul(
                        avp[0:65, :], v_hT[:, jt * 65: jt * 65 + 65],
                        Psb[psl][:, k * 512:(k + 1) * 512],
                        start=(k == 0), stop=(k == 3), skip_group_check=True),
                        reads=["v_hT", "v_ones", f"Psb{psl}"], writes=[avn])
                if g == 0:
                    P.op("DVE", lambda e, avp=avp: e.tensor_copy(
                        oacc[0:64, :], avp[0:64, :]), reads=[avn], writes=["oacc"])
                    P.op("DVE", lambda e, avp=avp: e.tensor_copy(
                        oacc[64:65, :], avp[64:65, :]), reads=[avn], writes=["oacc"])
                else:
                    P.op("DVE", lambda e, avp=avp: e.tensor_tensor(
                        oacc[0:64, :], oacc[0:64, :], avp[0:64, :], op=ALU.add),
                        reads=["oacc", avn], writes=["oacc"])
                    P.op("DVE", lambda e, avp=avp: e.tensor_tensor(
                        oacc[64:65, :], oacc[64:65, :], avp[64:65, :], op=ALU.add),
                        reads=["oacc", avn], writes=["oacc"])
            P.op("DVE", lambda e: e.reciprocal(rrow[0:1, :], oacc[64:65, :]),
                 reads=["oacc"], writes=["rrow"])
            P.op("PE", lambda e: e.matmul(pPB[0:64, :], ones64[0:1, :], rrow[0:1, :],
                                          start=True, stop=True, skip_group_check=True),
                 reads=["ones64", "rrow"], writes=["pPB"])
            ob = (h % 2) * 64
            P.op("DVE", lambda e, h=h, ob=ob, ic=ic: e.tensor_tensor(
                oT[ob:ob + 64, (h // 2) * T + ic * 512:(h // 2) * T + (ic + 1) * 512],
                oacc[0:64, :], pPB[0:64, :], op=ALU.mult),
                reads=["oacc", "pPB"], writes=["oT"])

    P.barrier()
    ar.reset(mZ)
    x2T = ar.take(8 * T, BF16)
    ar.reset(mX)
    wo_sb = ar.take(8 * D, BF16)
    x_sb = [ar.take(D, F32) for _ in range(2)]
    ln_s = ar.take(D, F32)
    sqd = ar.take(D, F32)
    x2f = [ar.take(D, F32) for _ in range(2)]
    st_sum = ar.take(1, F32)
    st_mean = ar.take(1, F32)
    st_ssq = ar.take(1, F32)
    st_t1 = ar.take(1, F32)
    st_rstd = ar.take(1, F32)
    assert ar.mark() <= mY, (ar.mark(), mY)

    P.op("QPOOL", lambda e: e.dma_start(out=wo_sb, in_=d_woT.rearrange("(dc p) j -> p dc j", p=128)),
         writes=["wo_sb"])
    P.op("QPOOL", lambda e: e.dma_start(out=gw, in_=d_gwT.rearrange("(dc p) e -> p dc e", p=128)),
         writes=["gwd2"])

    for ic in range(8):
        for t in range(4):
            t0 = ic * 512 + t * 128
            P.op("QSP", lambda e, t0=t0, t=t: e.dma_start(
                out=x_sb[t % 2], in_=d_x[t0:t0 + 128, :]),
                writes=[f"x_sb{t % 2}"])
            for jc in range(2):
                pb, pbn = (p1a, "p1a") if jc == 0 else (p1b, "p1b")
                for dc in range(8):
                    P.op("PE", lambda e, pb=pb, dc=dc, t0=t0, jc=jc: e.matmul(
                        pb, oT[:, dc * T + t0: dc * T + t0 + 128],
                        wo_sb[:, dc * D + jc * 512: dc * D + jc * 512 + 512],
                        start=(dc == 0), stop=(dc == 7), skip_group_check=True),
                        reads=["oT", "wo_sb", "gwd2"], writes=[pbn])
                P.op("DVE", lambda e, pb=pb, jc=jc: e.tensor_tensor(
                    ln_s[:, jc * 512:(jc + 1) * 512], pb,
                    bob[:, jc * 512:(jc + 1) * 512], op=ALU.add),
                    reads=[pbn, "bob"], writes=["ln_s"])
            P.op("DVE", lambda e, t=t: e.tensor_tensor(ln_s, ln_s, x_sb[t % 2], op=ALU.add),
                 reads=["ln_s", f"x_sb{t % 2}"], writes=["ln_s"])
            P.op("DVE", lambda e: e.reduce_sum(st_sum, ln_s, axis=AX.X),
                 reads=["ln_s"], writes=["st_sum"])
            P.op("DVE", lambda e: e.tensor_scalar_mul(st_mean, st_sum, 1.0 / D),
                 reads=["st_sum"], writes=["st_mean"])
            P.op("DVE", lambda e: e.tensor_scalar(ln_s, ln_s, st_mean, None, op0=ALU.subtract),
                 reads=["ln_s", "st_mean"], writes=["ln_s"])
            P.op("ACT", lambda e: e.activation(sqd, ln_s, AF.Square, accum_out=st_ssq),
                 reads=["ln_s"], writes=["sqd", "st_ssq"])
            P.op("ACT", lambda e: e.activation(st_t1, st_ssq, AF.Ln, bias=c_eps, scale=c_invD),
                 reads=["st_ssq", "c_eps", "c_invD"], writes=["st_t1"])
            P.op("ACT", lambda e: e.activation(st_rstd, st_t1, AF.Exp, scale=c_mhalf),
                 reads=["st_t1", "c_mhalf"], writes=["st_rstd"])
            xsl = t % 2
            P.op("DVE", lambda e, xsl=xsl: e.tensor_scalar(
                x2f[xsl], ln_s, st_rstd, None, op0=ALU.mult),
                reads=["ln_s", "st_rstd"], writes=[f"x2f{xsl}"])
            for dc in range(8):
                P.op("PE", lambda e, dc=dc, xsl=xsl: e.transpose(
                    pPB[:, 0:128], x2f[xsl][:, dc * 128:(dc + 1) * 128], ident),
                    reads=[f"x2f{xsl}", "ident"], writes=["pPB"])
                P.op("DVE", lambda e, dc=dc, t0=t0: e.tensor_copy(
                    x2T[:, dc * T + t0: dc * T + t0 + 128], pPB[:, 0:128]),
                    reads=["pPB"], writes=["x2T"])

    P.barrier()
    ar.reset(mX)
    w1h = ar.take(8 * 1024, BF16)
    w3h = ar.take(8 * 1024, BF16)
    w2h = ar.take(8 * D, BF16)
    h13 = ar.take(8 * 512, BF16)
    h1s = ar.take(512, BF16)
    assert ar.mark() <= mY, (ar.mark(), mY)
    ar.reset(mY)
    acc = ar.take(16 * D, BF16)
    x2res = ar.take(D, F32)
    l2_s = ar.take(D, F32)
    l2_sq = ar.take(D, F32)
    outf = ar.take(D, F32)
    g_e = ar.take(8, F32)
    g_eq = ar.take(8, F32)
    g_msk = ar.take(8, F32)
    g_ew = ar.take(8, F32)
    g_m1 = ar.take(1, F32)
    g_m2 = ar.take(1, F32)
    g_den = ar.take(1, F32)
    g_rden = ar.take(1, F32)
    l2_sum = ar.take(1, F32)
    l2_mean = ar.take(1, F32)
    l2_ssq = ar.take(1, F32)
    l2_t1 = ar.take(1, F32)
    l2_rstd = ar.take(1, F32)
    assert ar.mark() <= mZ, (ar.mark(), mZ)

    pPBbf = pPB[:, 0:64].bitcast(BF16)

    for hf in range(2):
        P.op("DVE", lambda e: e.memset(acc, 0.0), writes=["acc"])
        for t128 in range(16):
            t0 = hf * 2048 + t128 * 128
            for dc in range(8):
                P.op("PE", lambda e, dc=dc, t0=t0: e.matmul(
                    pPB[:, 0:8], x2T[:, dc * T + t0: dc * T + t0 + 128],
                    gw[:, dc * 8:(dc + 1) * 8],
                    start=(dc == 0), stop=(dc == 7), skip_group_check=True),
                    reads=["x2T", "gw"], writes=["pPB"])
            P.op("DVE", lambda e: e.reduce_max(g_m1, pPB[:, 0:8], axis=AX.X, negate=True),
                 reads=["pPB"], writes=["g_m1"])
            P.op("ACT", lambda e: e.activation(g_e, pPB[:, 0:8], AF.Exp, bias=g_m1),
                 reads=["pPB", "g_m1"], writes=["g_e"])
            P.op("DVE", lambda e: e.tensor_scalar(
                g_eq, pPB[:, 0:8], g_m1, 0.0, op0=ALU.add, op1=ALU.is_equal),
                reads=["pPB", "g_m1"], writes=["g_eq"])
            P.op("DVE", lambda e: e.scalar_tensor_tensor(
                g_msk, g_eq, -1e30, pPB[:, 0:8], op0=ALU.mult, op1=ALU.add),
                reads=["g_eq", "pPB"], writes=["g_msk"])
            P.op("DVE", lambda e: e.reduce_max(g_m2, g_msk, axis=AX.X, negate=True),
                 reads=["g_msk"], writes=["g_m2"])
            P.op("DVE", lambda e: e.tensor_scalar(
                g_msk, pPB[:, 0:8], g_m2, 0.0, op0=ALU.add, op1=ALU.is_ge),
                reads=["pPB", "g_m2"], writes=["g_msk"])
            P.op("DVE", lambda e: e.tensor_tensor(g_ew, g_e, g_msk, op=ALU.mult),
                 reads=["g_e", "g_msk"], writes=["g_ew"])
            P.op("DVE", lambda e: e.reduce_sum(g_den, g_ew, axis=AX.X),
                 reads=["g_ew"], writes=["g_den"])
            P.op("DVE", lambda e: e.reciprocal(g_rden, g_den),
                 reads=["g_den"], writes=["g_rden"])
            P.op("DVE", lambda e, t128=t128: e.tensor_scalar(
                wts[:, t128 * 8:(t128 + 1) * 8], g_ew, g_rden, None, op0=ALU.mult),
                reads=["g_ew", "g_rden"], writes=["wts"])
        for ex in range(NE):
            for fh in range(2):
                P.op("QPOOL", lambda e, ex=ex, fh=fh: e.dma_start(
                    out=w1h, in_=d_w1T[ex][:, fh * 1024:(fh + 1) * 1024]
                    .rearrange("(dc p) f -> p dc f", p=128)), writes=["w1h"])
                P.op("QPOOL", lambda e, ex=ex, fh=fh: e.dma_start(
                    out=w3h, in_=d_w3T[ex][:, fh * 1024:(fh + 1) * 1024]
                    .rearrange("(dc p) f -> p dc f", p=128)), writes=["w3h"])
                P.op("QPOOL", lambda e, ex=ex, fh=fh: e.dma_start(
                    out=w2h, in_=d_w2T[ex][fh * 1024:(fh + 1) * 1024, :]
                    .rearrange("(fc p) d -> p fc d", p=128)), writes=["w2h"])
                P.op("QPOOL", lambda e: e.dma_start(
                    out=gw, in_=d_gwT.rearrange("(dc p) e -> p dc e", p=128)),
                    writes=["gwd3"])
                for tc in range(4):
                    tb = hf * 2048 + tc * 512
                    for fc in range(8):
                        ph1, ph1n = (p2s[0], "p2a") if fc % 2 == 0 else (p2s[2], "p2c")
                        ph3, ph3n = (p2s[1], "p2b") if fc % 2 == 0 else (p2s[3], "p2d")
                        for dc in range(8):
                            P.op("PE", lambda e, ph1=ph1, fc=fc, dc=dc, tb=tb: e.matmul(
                                ph1, w1h[:, dc * 1024 + fc * 128: dc * 1024 + fc * 128 + 128],
                                x2T[:, dc * T + tb: dc * T + tb + 512],
                                start=(dc == 0), stop=(dc == 7), skip_group_check=True),
                                reads=["w1h", "w3h", "x2T"], writes=[ph1n])
                        P.op("ACT", lambda e, ph1=ph1: e.activation(h1s, ph1, AF.Silu),
                             reads=[ph1n], writes=["h1s"])
                        for dc in range(8):
                            P.op("PE", lambda e, ph3=ph3, fc=fc, dc=dc, tb=tb: e.matmul(
                                ph3, w3h[:, dc * 1024 + fc * 128: dc * 1024 + fc * 128 + 128],
                                x2T[:, dc * T + tb: dc * T + tb + 512],
                                start=(dc == 0), stop=(dc == 7), skip_group_check=True),
                                reads=["w3h", "w2h", "x2T"], writes=[ph3n])
                        P.op("DVE", lambda e, ph3=ph3, fc=fc: e.tensor_tensor(
                            h13[:, fc * 512:(fc + 1) * 512], h1s, ph3, op=ALU.mult),
                            reads=["h1s", ph3n], writes=["h13"])
                    for tt in range(4):
                        t128 = tc * 4 + tt
                        for jc in range(2):
                            pw, pwn = (p1a, "p1a") if jc == 0 else (p1b, "p1b")
                            for fc in range(8):
                                P.op("PE", lambda e, pw=pw, fc=fc, tt=tt, jc=jc: e.matmul(
                                    pw, h13[:, fc * 512 + tt * 128: fc * 512 + tt * 128 + 128],
                                    w2h[:, fc * D + jc * 512: fc * D + jc * 512 + 512],
                                    start=(fc == 0), stop=(fc == 7), skip_group_check=True),
                                    reads=["h13", "w2h", "gwd3"], writes=[pwn])
                            P.op("DVE", lambda e, pw=pw, t128=t128, jc=jc, ex=ex: e.scalar_tensor_tensor(
                                acc[:, t128 * D + jc * 512: t128 * D + jc * 512 + 512],
                                pw, wts[:, t128 * 8 + ex: t128 * 8 + ex + 1],
                                acc[:, t128 * D + jc * 512: t128 * D + jc * 512 + 512],
                                op0=ALU.mult, op1=ALU.add),
                                reads=[pwn, "wts", "acc"], writes=["acc"])
        for t128 in range(16):
            row0 = hf * 2048 + t128 * 128
            for dc in range(8):
                P.op("PE", lambda e, dc=dc, row0=row0: e.transpose(
                    pPBbf, x2T[:, dc * T + row0: dc * T + row0 + 128], identbf),
                    reads=["x2T", "identbf"], writes=["pPB"])
                P.op("DVE", lambda e, dc=dc: e.tensor_copy(
                    x2res[:, dc * 128:(dc + 1) * 128], pPBbf),
                    reads=["pPB"], writes=["x2res"])
            P.op("DVE", lambda e, t128=t128: e.tensor_tensor(
                l2_s, acc[:, t128 * D:(t128 + 1) * D], x2res, op=ALU.add),
                reads=["acc", "x2res"], writes=["l2_s"])
            P.op("DVE", lambda e: e.reduce_sum(l2_sum, l2_s, axis=AX.X),
                 reads=["l2_s"], writes=["l2_sum"])
            P.op("DVE", lambda e: e.tensor_scalar_mul(l2_mean, l2_sum, 1.0 / D),
                 reads=["l2_sum"], writes=["l2_mean"])
            P.op("DVE", lambda e: e.tensor_scalar(l2_s, l2_s, l2_mean, None, op0=ALU.subtract),
                 reads=["l2_s", "l2_mean"], writes=["l2_s"])
            P.op("ACT", lambda e: e.activation(l2_sq, l2_s, AF.Square, accum_out=l2_ssq),
                 reads=["l2_s"], writes=["l2_sq", "l2_ssq"])
            P.op("ACT", lambda e: e.activation(l2_t1, l2_ssq, AF.Ln, bias=c_eps, scale=c_invD),
                 reads=["l2_ssq", "c_eps", "c_invD"], writes=["l2_t1"])
            P.op("ACT", lambda e: e.activation(l2_rstd, l2_t1, AF.Exp, scale=c_mhalf),
                 reads=["l2_t1", "c_mhalf"], writes=["l2_rstd"])
            P.op("DVE", lambda e: e.tensor_scalar(
                outf, l2_s, l2_rstd, None, op0=ALU.mult),
                reads=["l2_s", "l2_rstd"], writes=["outf"])
            P.op("QSP", lambda e, row0=row0: e.dma_start(
                out=d_out[row0:row0 + 128, :], in_=outf),
                reads=["outf"], writes=["OUT"])

    P.op("SP", lambda e: e.nop(), reads=["OUT"])

    sem_ctx = P.emit_all(nc)
    ctx.enter_context(sem_ctx)
    return nc, ctx


def _prep_inputs(L0_attn_x, L0_attn_Wq, L0_attn_bq, L0_attn_Wk, L0_attn_bk,
                 L0_attn_Wv, L0_attn_bv, L0_attn_Wo, L0_attn_bo, L0_attn_scale,
                 L0_ffn_gate_w, L0_ffn_w1, L0_ffn_w2, L0_ffn_w3):
    f = np.float32
    bf = ml_dtypes.bfloat16
    x = np.asarray(L0_attn_x, f)
    Wq = np.asarray(L0_attn_Wq, f)
    Wk = np.asarray(L0_attn_Wk, f)
    Wv = np.asarray(L0_attn_Wv, f)
    Wo = np.asarray(L0_attn_Wo, f)
    bq = np.asarray(L0_attn_bq, f)
    bk = np.asarray(L0_attn_bk, f)
    bv = np.asarray(L0_attn_bv, f)
    bo = np.asarray(L0_attn_bo, f)
    scale = np.asarray(L0_attn_scale, f).reshape(-1)[0]
    gwm = np.asarray(L0_ffn_gate_w, f)
    w1 = np.asarray(L0_ffn_w1, f)
    w2 = np.asarray(L0_ffn_w2, f)
    w3 = np.asarray(L0_ffn_w3, f)

    perm = np.arange(D) ^ 1
    cosT, sinT = _rope_tables()
    mask2 = _masks()

    common = {
        "wqT": np.ascontiguousarray(Wq.T),
        "wqsT": np.ascontiguousarray(Wq.T[:, perm]),
        "wkT": np.ascontiguousarray(Wk.T),
        "wksT": np.ascontiguousarray(Wk.T[:, perm]),
        "wvT": np.ascontiguousarray(Wv.T),
        "woT": np.ascontiguousarray(Wo.T),
        "bqh": _bias_heads(bq),
        "bqsh": _bias_heads(bq[perm]),
        "bkh": _bias_heads(bk),
        "bksh": _bias_heads(bk[perm]),
        "bv_b": np.ascontiguousarray(np.broadcast_to(bv, (128, D))),
        "bo_b": np.ascontiguousarray(np.broadcast_to(bo, (128, D))),
        "scale_rep": np.full((128, 1), scale, f),
        "gwT": np.ascontiguousarray(gwm.T),
        "w1T": np.ascontiguousarray(w1.transpose(0, 2, 1)),
        "w3T": np.ascontiguousarray(w3.transpose(0, 2, 1)),
        "w2T": np.ascontiguousarray(w2.transpose(0, 2, 1)),
        "cosT": cosT.astype(bf), "sinT": sinT.astype(bf),
        "mask2": mask2.astype(bf),
        "ident": np.eye(128, dtype=f),
        "identbf": np.eye(128).astype(bf),
    }
    in_maps = []
    for b in range(B):
        m = dict(common)
        m["x_tm"] = np.ascontiguousarray(x[b])
        m["xT"] = np.ascontiguousarray(x[b].T)
        in_maps.append(m)
    return in_maps


def kernel(**inputs):
    in_maps = _prep_inputs(**inputs)
    nc, ctx = build_program()
    with ctx:
        res = run_bass_kernel_spmd(nc, in_maps, core_ids=list(range(B)))
    out = np.stack([np.asarray(r["out"], dtype=np.float32) for r in res.results])
    return out
